# version 14
# speedup vs baseline: 1.6443x; 1.0321x over previous
import numpy as np
from contextlib import ExitStack

import concourse.bass as bass
import concourse.mybir as mybir
from concourse.bass_utils import run_bass_kernel_spmd

F32 = mybir.dt.float32
BF16 = mybir.dt.bfloat16
AF = mybir.ActivationFunctionType
ALU = mybir.AluOpType
AX = mybir.AxisListType

B, S, D, H, N = 4, 2048, 2048, 8192, 16
TOK = 1024
NTB = TOK // 128
NDC = D // 128
NHB = H // 128
CH = 1024
NEB = CH // 128
TTC = 256
NTT = S // TTC
QTOK = 256
NQ = TOK // QTOK
LN_E1 = 1e-5
LN_E2 = 1e-8
DT_MIN = 1e-4
LBITS = [128.0, 64.0, 32.0, 16.0, 8.0, 4.0, 2.0, 1.0]
NBITS = len(LBITS)

ENGINES = ("sync", "scalar", "vector", "tensor", "gpsimd")


class Sem:
    def __init__(self, name):
        self.name = name
        self.val = 0
        self.obj = None


def build_nc(stage=3):
    nc = bass.Bass("TRN2", target_bir_lowering=False, debug=False)

    x_in = nc.dram_tensor("x", [TOK, D], F32, kind="ExternalInput").ap()
    wg_in = nc.dram_tensor("wg", [H, D], F32, kind="ExternalInput").ap()
    wu_in = nc.dram_tensor("wu", [H, D], F32, kind="ExternalInput").ap()
    wd_in = nc.dram_tensor("wd", [D, H], F32, kind="ExternalInput").ap()
    wdt_in = nc.dram_tensor("wdt", [D, CH], F32, kind="ExternalInput").ap()
    wb_in = nc.dram_tensor("wb", [D, CH], F32, kind="ExternalInput").ap()
    wc_in = nc.dram_tensor("wc", [D, CH], F32, kind="ExternalInput").ap()
    bdt_in = nc.dram_tensor("bdt", [CH], F32, kind="ExternalInput").ap()
    bb_in = nc.dram_tensor("bb", [CH], F32, kind="ExternalInput").ap()
    bc_in = nc.dram_tensor("bc", [CH], F32, kind="ExternalInput").ap()
    araw_in = nc.dram_tensor("araw", [CH, N], F32, kind="ExternalInput").ap()
    bbase_in = nc.dram_tensor("bbase", [CH, N], F32, kind="ExternalInput").ap()
    cbase_in = nc.dram_tensor("cbase", [CH, N], F32, kind="ExternalInput").ap()
    dvec_in = nc.dram_tensor("dvec", [CH], F32, kind="ExternalInput").ap()
    id16_in = nc.dram_tensor("id16", [128, 128], BF16, kind="ExternalInput").ap()
    id32_in = nc.dram_tensor("id32", [128, 128], F32, kind="ExternalInput").ap()
    hmask_in = nc.dram_tensor("hmask", [128, 2], F32, kind="ExternalInput").ap()
    out_d = nc.dram_tensor("out", [TOK, D], F32, kind="ExternalOutput").ap()

    wg_s = nc.dram_tensor("wg_s", [H, D], BF16).ap()
    wu_s = nc.dram_tensor("wu_s", [H, D], BF16).ap()
    wd_s = nc.dram_tensor("wd_s", [D, H], BF16).ap()
    pdt_s = nc.dram_tensor("pdt_s", [D, CH], BF16).ap()
    pb_s = nc.dram_tensor("pb_s", [D, CH], BF16).ap()
    pc_s = nc.dram_tensor("pc_s", [D, CH], BF16).ap()
    lnt_i = nc.dram_tensor("lnt_i", [4, D, 256], F32).ap()
    lnt_o = nc.dram_tensor("lnt_o", [4, 2, D, 256], F32).ap()
    dt_s = nc.dram_tensor("dt_s", [CH, S], F32).ap()
    b_s = nc.dram_tensor("b_s", [CH, S], F32).ap()
    c_s = nc.dram_tensor("c_s", [CH, S], F32).ap()
    y_i = nc.dram_tensor("y_i", [NEB, 128, S], F32).ap()
    y_o = nc.dram_tensor("y_o", [NEB, 2, 128, S], F32).ap()

    steps = {e: [] for e in ENGINES}
    sT, sV, sS = Sem("sT"), Sem("sV"), Sem("sS")
    sCC, sOUT, sB = Sem("sCC"), Sem("sOUT"), Sem("sB")
    sGPC = Sem("sGPC")
    sINIT = Sem("sINIT")
    sPL = [Sem("sPL0"), Sem("sPL1"), Sem("sPL2")]
    sPS = [Sem("sPS0"), Sem("sPS1"), Sem("sPS2")]
    sWP = [Sem("sWP0"), Sem("sWP1")]
    sDP = [Sem("sDP0"), Sem("sDP1")]
    sPW = [Sem("sPW0"), Sem("sPW1")]
    sPCAST, sUL = Sem("sPCAST"), Sem("sUL")
    sRHSp = [Sem("sRHS0"), Sem("sRHS1")]
    sLNT, sYST, sYLD = Sem("sLNT"), Sem("sYST"), Sem("sYLD")
    sPJS = [Sem("sPJS0"), Sem("sPJS1")]
    sDT = [Sem("sDT0"), Sem("sDT1")]
    dma_sems_sync = [sINIT, sPL[0], sPL[1], sPL[2], sWP[0], sWP[1],
                     sDP[0], sDP[1], sPW[0], sPW[1], sUL, sYLD, sOUT,
                     sDT[0], sDT[1]]
    dma_sems_scalar = [sLNT, sYST, sPJS[0], sPJS[1], sPS[0], sPS[1], sPS[2]]
    dma_sems_gp = [sPCAST, sRHSp[0], sRHSp[1]]
    all_sems = ([sT, sV, sS, sCC, sB, sGPC] + dma_sems_sync + dma_sems_scalar
                + dma_sems_gp)

    def on(eng):
        def reg(fn):
            steps[eng].append(fn)
        return reg

    def wait(eng, sem, val=None):
        v = sem.val if val is None else val
        if v <= 0:
            return
        @on(eng)
        def _(E, v=v, sem=sem):
            E.wait_ge(sem.obj, v)

    ctx = ExitStack()

    def sb(name, shape, dt, where=None):
        return (where or ctx).enter_context(nc.sbuf_tensor(name, shape, dt))

    def psum(name, shape):
        return ctx.enter_context(nc.psum_tensor(name, shape, F32))

    def full_barrier():
        for sm in dma_sems_sync:
            wait("sync", sm, sm.val)
        for sm in dma_sems_scalar:
            wait("scalar", sm, sm.val)

        @on("sync")
        def _(E):
            E.nop().then_inc(sB.obj, 1)

        @on("scalar")
        def _(E):
            E.drain().then_inc(sB.obj, 1)

        @on("vector")
        def _(E):
            E.drain().then_inc(sB.obj, 1)

        @on("tensor")
        def _(E):
            E.drain().then_inc(sB.obj, 1)

        for sm in dma_sems_gp:
            wait("gpsimd", sm, sm.val)
        wait("gpsimd", sCC, sCC.val)

        @on("gpsimd")
        def _(E):
            E.drain().then_inc(sB.obj, 1)
        sB.val += 5
        for e in ENGINES:
            wait(e, sB, sB.val)

    x_tm = sb("x_tm", [128, NTB * D], F32)
    consts = sb("consts", [128, 16], F32)
    bdt_t = sb("bdt_t", [128, NEB], F32)
    bb_t = sb("bb_t", [128, NEB], F32)
    bc_t = sb("bc_t", [128, NEB], F32)
    dv_t = sb("dv_t", [128, NEB], F32)
    a_t = sb("a_t", [128, NEB * N], F32)
    btl_t = sb("btl_t", [128, NEB * N], F32)
    id16_t = sb("id16_t", [128, 128], BF16)
    id32_t = sb("id32_t", [128, 128], F32)
    hmask_t = sb("hmask_t", [128, 2], F32)
    stat_t = sb("stat_t", [128, 24], F32)
    mv_t = sb("mv_t", [128, 2], F32)
    sc1_t = sb("sc1_t", [128, 8], F32)
    z_t = sb("z_t", [128, D], F32)
    st_t = sb("st_t", [128, NEB * N], F32)
    prm_ctx = ExitStack()
    araw_t = sb("araw_t", [128, NEB * N], F32, prm_ctx)
    bbase_t = sb("bbase_t", [128, NEB * N], F32, prm_ctx)
    cbase_t = sb("cbase_t", [128, NEB * N], F32, prm_ctx)

    HPG = 16
    NHG = NHB // HPG

    class FfnBufs:
        def __init__(self, tag, where, mm=True):
            self.tag = tag
            self.u0_t = sb(f"u0_t{tag}", [128, D], F32, where)
            self.uw_t = sb(f"uw_t{tag}", [128, D], F32, where)
            self.xqT = sb(f"xqT{tag}", [128, NDC * TOK], BF16, where)
            if mm:
                self.alloc_mm(where)

        def alloc_mm(self, where):
            tag = self.tag
            self.p_pool = sb(f"p_pool{tag}", [128, HPG * TOK], BF16, where)
            self.wgp = [sb(f"wgp{tag}_{i}", [128, D], BF16, where) for i in range(2)]
            self.wup = [sb(f"wup{tag}_{i}", [128, D], BF16, where) for i in range(2)]
            self.wdp = [sb(f"wdp{tag}_{i}", [128, 8 * 512], BF16, where) for i in range(2)]
            self.gtmp = sb(f"gtmp{tag}", [128, 512], F32, where)

    ps_d = [psum(f"ps_d{i}", [128, 512]) for i in range(4)]
    ps_g = [psum(f"ps_g{i}", [128, 512]) for i in range(4)]

    CONES = NBITS

    @on("sync")
    def _(E):
        for tb in range(NTB):
            E.dma_start(out=x_tm[:, tb * D:(tb + 1) * D],
                        in_=x_in[tb * 128:(tb + 1) * 128, :]).then_inc(sINIT.obj, 16)
        E.dma_start(out=bdt_t[:], in_=bdt_in.rearrange("(e p) -> p e", p=128), allow_slow_non_contiguous=True).then_inc(sINIT.obj, 16)
        E.dma_start(out=bb_t[:], in_=bb_in.rearrange("(e p) -> p e", p=128), allow_slow_non_contiguous=True).then_inc(sINIT.obj, 16)
        E.dma_start(out=bc_t[:], in_=bc_in.rearrange("(e p) -> p e", p=128), allow_slow_non_contiguous=True).then_inc(sINIT.obj, 16)
        E.dma_start(out=dv_t[:], in_=dvec_in.rearrange("(e p) -> p e", p=128), allow_slow_non_contiguous=True).then_inc(sINIT.obj, 16)
        E.dma_start(out=araw_t[:].rearrange("p (e n) -> p e n", n=N), in_=araw_in.rearrange("(e p) n -> p e n", p=128), allow_slow_non_contiguous=True).then_inc(sINIT.obj, 16)
        E.dma_start(out=bbase_t[:].rearrange("p (e n) -> p e n", n=N), in_=bbase_in.rearrange("(e p) n -> p e n", p=128), allow_slow_non_contiguous=True).then_inc(sINIT.obj, 16)
        E.dma_start(out=cbase_t[:].rearrange("p (e n) -> p e n", n=N), in_=cbase_in.rearrange("(e p) n -> p e n", p=128), allow_slow_non_contiguous=True).then_inc(sINIT.obj, 16)
        E.dma_start(out=id16_t[:], in_=id16_in).then_inc(sINIT.obj, 16)
        E.dma_start(out=id32_t[:], in_=id32_in).then_inc(sINIT.obj, 16)
        E.dma_start(out=hmask_t[:], in_=hmask_in).then_inc(sINIT.obj, 16)
    sINIT.val += 16 * (NTB + 10)
    INIT_LOADED = sINIT.val

    @on("vector")
    def _(E):
        carry = 0.0
        for i, bbit in enumerate(LBITS):
            E.memset(consts[:, i:i + 1], -bbit - carry)
            carry += bbit / 2.0
        E.memset(consts[:, CONES:CONES + 1], 1.0)
        E.drain().then_inc(sV.obj, 1)
    sV.val += 1

    wait("scalar", sINIT, INIT_LOADED)
    wait("scalar", sV, sV.val)

    @on("scalar")
    def _(E):
        E.activation(a_t[:], araw_t[:], AF.Abs, bias=0.0, scale=1.0)
        E.drain()
        E.activation(a_t[:], a_t[:], AF.Exp, bias=0.0, scale=-1.0)
        E.drain()
        E.activation(a_t[:], a_t[:], AF.Ln, bias=consts[:, CONES:CONES + 1], scale=1.0)
        E.drain()
        E.activation(btl_t[:], araw_t[:], AF.Relu, bias=0.0, scale=1.0)
        E.drain().then_inc(sS.obj, 1)
    sS.val += 1

    wait("vector", sS, sS.val)
    wait("vector", sINIT, INIT_LOADED)

    @on("vector")
    def _(E):
        E.tensor_tensor(a_t[:], a_t[:], btl_t[:], ALU.add)
        E.drain()
        E.tensor_scalar(a_t[:], a_t[:], -1.0, None, ALU.mult)
        E.drain()
        E.tensor_tensor(btl_t[:], cbase_t[:], bbase_t[:], ALU.mult)
        E.drain().then_inc(sV.obj, 1)
    sV.val += 1

    full_barrier()
    prm_ctx.close()


    def emit_layernorm(tb, mode):
        @on("vector")
        def _(E, tb=tb, mode=mode):
            for k in range(4):
                E.bn_stats(stat_t[:, k * 6:(k + 1) * 6],
                           x_tm[:, tb * D + k * 512: tb * D + (k + 1) * 512])
            E.drain()
            E.bn_aggr(mv_t[:], stat_t[:])
            E.drain()
            if mode == "ffn":
                E.tensor_scalar(sc1_t[:, 1:2], mv_t[:, 1:2], 1.0 + LN_E2,
                                LN_E2 * LN_E1, ALU.mult, ALU.add)
            else:
                E.tensor_scalar(sc1_t[:, 1:2], mv_t[:, 1:2], 1.0, LN_E1,
                                ALU.mult, ALU.add)
            E.drain().then_inc(sV.obj, 1)
        sV.val += 1

        wait("scalar", sV, sV.val)
        @on("scalar")
        def _(E):
            E.activation(sc1_t[:, 2:3], sc1_t[:, 1:2], AF.Sqrt, bias=0.0, scale=1.0)
            E.drain().then_inc(sS.obj, 1)
        sS.val += 1

        wait("vector", sS, sS.val)
        @on("vector")
        def _(E, tb=tb):
            E.reciprocal(sc1_t[:, 0:1], sc1_t[:, 2:3])
            E.drain()
            E.tensor_scalar(z_t[:], x_tm[:, tb * D:(tb + 1) * D],
                            mv_t[:, 0:1], sc1_t[:, 0:1], ALU.subtract, ALU.mult)
            E.drain().then_inc(sV.obj, 1)
        sV.val += 1

    gu_done = [0, 0]
    gu_mm_done = []
    dn_mm_done = []
    dn_ev_done = [0, 0]

    def make_quant(bufs):
        u0_t, uw_t, xqT = bufs.u0_t, bufs.uw_t, bufs.xqT
        m_t = z_t

        def emit_quant(tb):
            wait("vector", sT, sT.val)
            emit_layernorm(tb, "ffn")

            @on("vector")
            def _(E):
                E.tensor_reduce(sc1_t[:, 3:4], z_t[:], AX.X, ALU.max,
                                apply_absolute_value=True)
                E.drain()
                E.reciprocal(sc1_t[:, 4:5], sc1_t[:, 3:4])
                E.drain()
                E.tensor_scalar(sc1_t[:, 5:6], sc1_t[:, 4:5], 127.0, None, ALU.mult)
                E.tensor_scalar(sc1_t[:, 6:7], sc1_t[:, 3:4], 1.0 / 127.0, None, ALU.mult)
                E.drain()
                E.tensor_scalar(u0_t[:], z_t[:], sc1_t[:, 5:6], 128.5, ALU.mult, ALU.add)
                E.drain()
                E.tensor_copy(uw_t[:], u0_t[:])
                E.drain().then_inc(sV.obj, 1)
            sV.val += 1

            for i in range(NBITS):
                wait("scalar", sV, sV.val)
                @on("scalar")
                def _(E, i=i):
                    E.activation(m_t[:], uw_t[:], AF.Sign,
                                 bias=consts[:, i:i + 1], scale=1.0)
                    E.drain().then_inc(sS.obj, 1)
                sS.val += 1

                wait("vector", sS, sS.val)
                @on("vector")
                def _(E, i=i):
                    E.scalar_tensor_tensor(uw_t[:], m_t[:], -LBITS[i] / 2.0,
                                           uw_t[:], ALU.mult, ALU.add)
                    E.drain().then_inc(sV.obj, 1)
                sV.val += 1

            @on("vector")
            def _(E):
                E.scalar_tensor_tensor(m_t[:], uw_t[:], -1.0, u0_t[:],
                                       ALU.mult, ALU.add)
                E.drain()
                E.tensor_scalar(m_t[:], m_t[:], -0.5, -128.0, ALU.add, ALU.max)
                E.drain()
                E.tensor_scalar(u0_t[:], m_t[:], 127.0, sc1_t[:, 6:7],
                                ALU.min, ALU.mult)
                E.drain().then_inc(sV.obj, 1)
            sV.val += 1
            XQ_OK = sV.val

            evac_done = []
            for dc in range(NDC):
                pt = ps_g[dc % 2]
                wait("tensor", sV, XQ_OK)
                wait("tensor", sV, max(gu_done))
                if dc >= 2:
                    wait("tensor", sS, evac_done[dc - 2])
                @on("tensor")
                def _(E, dc=dc, pt=pt):
                    E.transpose(pt[:, 0:128], u0_t[:, dc * 128:(dc + 1) * 128],
                                id32_t[:]).then_inc(sT.obj, 1)
                sT.val += 1

                wait("scalar", sT, sT.val)
                @on("scalar")
                def _(E, dc=dc, pt=pt, tb=tb):
                    E.copy(xqT[:, dc * TOK + tb * 128: dc * TOK + (tb + 1) * 128],
                           pt[:, 0:128])
                    E.drain().then_inc(sS.obj, 1)
                sS.val += 1
                evac_done.append(sS.val)
        return emit_quant

    def emit_ffn(call, bufs, emit_quant, xqt0=None, xqt1=None):
        u0_t, uw_t = bufs.u0_t, bufs.uw_t
        xqT, p_pool, wgp, wup, wdp = (bufs.xqT, bufs.p_pool, bufs.wgp,
                                      bufs.wup, bufs.wdp)
        gtmp = bufs.gtmp
        if xqt0 is None:
            emit_quant(0)
            emit_quant(1)
            emit_quant(2)
            emit_quant(3)
            xqt0 = sS.val
        xqt_half = [xqt0, xqt1]

        for hgrp in range(NHG):
            for th in range(2):
                for slot in range(HPG):
                    hb = hgrp * HPG + slot
                    if len(gu_mm_done) >= 2:
                        wait("sync", sT, gu_mm_done[-2])
                    psem = sWP[slot % 2]
                    @on("sync")
                    def _(E, hb=hb, slot=slot, psem=psem):
                        E.dma_start(
                            out=wgp[slot % 2][:].rearrange("p (m h) -> p m h", m=NDC),
                            in_=wg_s[hb * 128:(hb + 1) * 128, :], transpose=True,
                        ).then_inc(psem.obj, 16)
                        E.dma_start(
                            out=wup[slot % 2][:].rearrange("p (m h) -> p m h", m=NDC),
                            in_=wu_s[hb * 128:(hb + 1) * 128, :], transpose=True,
                        ).then_inc(psem.obj, 16)
                    psem.val += 32
                    PANELS = psem.val

                    psG = ps_g[2 + (slot % 2)]
                    psU = ps_g[slot % 2]
                    wait("tensor", psem, PANELS)
                    wait("tensor", sS, xqt_half[th])
                    wait("tensor", sV, gu_done[slot % 2])
                    @on("tensor")
                    def _(E, slot=slot, th=th, psG=psG, psU=psU):
                        for dc in range(NDC):
                            E.matmul(psG[:, 0:512],
                                     wgp[slot % 2][:, dc * 128:(dc + 1) * 128],
                                     xqT[:, dc * TOK + th * 512: dc * TOK + (th + 1) * 512],
                                     start=(dc == 0), stop=(dc == NDC - 1))
                        mm = None
                        for dc in range(NDC):
                            mm = E.matmul(psU[:, 0:512],
                                          wup[slot % 2][:, dc * 128:(dc + 1) * 128],
                                          xqT[:, dc * TOK + th * 512: dc * TOK + (th + 1) * 512],
                                          start=(dc == 0), stop=(dc == NDC - 1))
                        mm.then_inc(sT.obj, 1)
                    sT.val += 1
                    gu_mm_done.append(sT.val)

                    wait("scalar", sT, sT.val)
                    wait("scalar", sV, gu_done[(slot + 1) % 2])
                    @on("scalar")
                    def _(E, psG=psG):
                        E.activation(gtmp[:], psG[:, 0:512], AF.Silu,
                                     bias=0.0, scale=1.0)
                        E.drain().then_inc(sS.obj, 1)
                    sS.val += 1

                    wait("vector", sS, sS.val)
                    @on("vector")
                    def _(E, slot=slot, th=th, psU=psU):
                        E.tensor_tensor(p_pool[:, slot * TOK + th * 512: slot * TOK + (th + 1) * 512],
                                        gtmp[:], psU[:, 0:512], ALU.mult)
                        E.drain().then_inc(sV.obj, 1)
                    sV.val += 1
                    gu_done[slot % 2] = sV.val
                if hgrp == 0 and th == 0 and xqt_half[1] is None:
                    emit_quant(4)
                    emit_quant(5)
                    emit_quant(6)
                    emit_quant(7)
                    xqt_half[1] = sS.val
            P_READY = sV.val

            for dgrp in range(4):
                WD_OKS = []
                for wh in range(2):
                    if dn_mm_done and (dgrp > 0 or wh > 0 or hgrp > 0):
                        wait("sync", sT, dn_mm_done[-1])
                    dsem = sDP[wh]
                    @on("sync")
                    def _(E, hgrp=hgrp, dgrp=dgrp, wh=wh, dsem=dsem):
                        E.dma_start(
                            out=wdp[wh][:].rearrange("p (m h) -> p m h", m=8),
                            in_=wd_s[dgrp * 512:(dgrp + 1) * 512,
                                     hgrp * 2048 + wh * 1024: hgrp * 2048 + (wh + 1) * 1024],
                            transpose=True,
                        ).then_inc(dsem.obj, 16)
                    dsem.val += 16
                    WD_OKS.append(dsem.val)

                ev8 = []
                for ti in range(8):
                    th, tb2 = ti // 4, ti % 4
                    tb = th * 4 + tb2
                    pd = ps_d[2 + (ti % 2)]
                    wait("tensor", sDP[0], WD_OKS[0])
                    wait("tensor", sDP[1], WD_OKS[1])
                    if ti == 0:
                        wait("tensor", sV, P_READY)
                    if ti >= 2:
                        wait("tensor", sV, ev8[ti - 2])
                    elif dn_ev_done[ti % 2]:
                        wait("tensor", sV, dn_ev_done[ti % 2])
                    @on("tensor")
                    def _(E, th=th, tb2=tb2, dgrp=dgrp, pd=pd):
                        mm = None
                        for slot in range(HPG):
                            mm = E.matmul(
                                pd[:, 0:512],
                                p_pool[:, slot * TOK + th * 512 + tb2 * 128:
                                       slot * TOK + th * 512 + (tb2 + 1) * 128],
                                wdp[slot // 8][:, (slot % 8) * 512:(slot % 8 + 1) * 512],
                                start=(slot == 0), stop=(slot == HPG - 1),
                            )
                        mm.then_inc(sT.obj, 1)
                    sT.val += 1
                    if ti == 7:
                        dn_mm_done.append(sT.val)

                    wait("vector", sT, sT.val)
                    @on("vector")
                    def _(E, tb=tb, dgrp=dgrp, pd=pd):
                        E.scalar_tensor_tensor(
                            x_tm[:, tb * D + dgrp * 512: tb * D + (dgrp + 1) * 512],
                            pd[:, 0:512], 0.5,
                            x_tm[:, tb * D + dgrp * 512: tb * D + (dgrp + 1) * 512],
                            ALU.mult, ALU.add)
                        E.drain().then_inc(sV.obj, 1)
                    sV.val += 1
                    ev8.append(sV.val)
                dn_ev_done[0] = ev8[-2]
                dn_ev_done[1] = ev8[-1]

    f1_ctx = ExitStack()
    f1_bufs = FfnBufs("f1", f1_ctx, mm=False)
    f1_quant = make_quant(f1_bufs)

    @on("gpsimd")
    def _(E):
        E.dma_start(out=pdt_s, in_=wdt_in).then_inc(sPCAST.obj, 16)
        E.dma_start(out=pb_s, in_=wb_in).then_inc(sPCAST.obj, 16)
        E.dma_start(out=pc_s, in_=wc_in).then_inc(sPCAST.obj, 16)
    sPCAST.val += 48
    PROJ_CAST = sPCAST.val

    pp_ctx = ExitStack()
    pp_in = [sb(f"pp_in{i}", [128, D], F32, pp_ctx) for i in range(3)]
    pp_out = [sb(f"pp_out{i}", [128, D], BF16, pp_ctx) for i in range(3)]
    pp_jobs = (
        [(wg_in[r * 128:(r + 1) * 128, :], wg_s[r * 128:(r + 1) * 128, :], D) for r in range(NHB)]
        + [(wu_in[r * 128:(r + 1) * 128, :], wu_s[r * 128:(r + 1) * 128, :], D) for r in range(NHB)]
        + [(wd_in[r * 128:(r + 1) * 128, cc * D:(cc + 1) * D],
            wd_s[r * 128:(r + 1) * 128, cc * D:(cc + 1) * D], D)
           for r in range(NDC) for cc in range(H // D)]
    )
    pp_sign_done = []
    _PPQ = len(pp_jobs) // 5
    _xqt_marks = {}
    for i, (src, dst, cols) in enumerate(pp_jobs):
        if i % _PPQ == 0 and i // _PPQ < 4:
            k = i // _PPQ
            f1_quant(k * 2)
            f1_quant(k * 2 + 1)
            _xqt_marks[k] = sS.val
        buf, obuf = pp_in[i % 3], pp_out[i % 3]
        lsem, ssem2 = sPL[i % 3], sPS[i % 3]
        if i >= 3:
            wait("sync", sS, pp_sign_done[i - 3])
        @on("sync")
        def _(E, src=src, buf=buf, cols=cols, lsem=lsem):
            E.dma_start(out=buf[:, 0:cols], in_=src).then_inc(lsem.obj, 16)
        lsem.val += 16

        wait("scalar", lsem, lsem.val)
        wait("scalar", ssem2, ssem2.val)
        @on("scalar")
        def _(E, buf=buf, obuf=obuf, cols=cols, dst=dst, ssem2=ssem2):
            E.activation(obuf[:, 0:cols], buf[:, 0:cols], AF.Sign, bias=0.0, scale=1.0)
            E.drain().then_inc(sS.obj, 1)
            E.dma_start(out=dst, in_=obuf[:, 0:cols]).then_inc(ssem2.obj, 16)
        sS.val += 1
        ssem2.val += 16
        pp_sign_done.append(sS.val)
    full_barrier()
    pp_ctx.close()
    f1_bufs.alloc_mm(f1_ctx)

    if stage >= 1:
        emit_ffn(0, f1_bufs, f1_quant,
                 xqt0=_xqt_marks[1], xqt1=_xqt_marks[3])
    full_barrier()
    f1_ctx.close()

    m_ctx = ExitStack()
    mp_ctx = ExitStack()
    z2slab = sb("z2slab", [128, D], F32, mp_ctx)
    rhsT2 = [sb(f"rhsT{i}", [128, NDC * 512], BF16, mp_ctx) for i in range(2)]
    pwfull = [sb(f"pwfull{i}", [128, NDC * CH // 8 * 8], BF16, mp_ctx) for i in range(2)]
    pstage = [sb(f"pstage{i}", [128, 512], F32, mp_ctx) for i in range(2)]
    t3_t = sb("t3_t", [128, 512], F32, mp_ctx)
    t4_t = sb("t4_t", [128, 512], F32, mp_ctx)
    if stage >= 2:
        agA = []
        for tb in range(NTB):
            wait("vector", sT, sT.val)
            if tb > 0:
                wait("scalar", sLNT, sLNT.val)
            emit_layernorm(tb, "ln")
            LN_OK = sV.val
            ev_done2 = []
            for dc in range(NDC):
                pt = ps_g[dc % 2]
                wait("tensor", sV, LN_OK)
                if dc >= 2:
                    wait("tensor", sS, ev_done2[dc - 2])
                @on("tensor")
                def _(E, dc=dc, pt=pt):
                    E.transpose(pt[:, 0:128], z_t[:, dc * 128:(dc + 1) * 128],
                                id32_t[:]).then_inc(sT.obj, 1)
                sT.val += 1

                wait("scalar", sT, sT.val)
                @on("scalar")
                def _(E, dc=dc, pt=pt):
                    E.copy(z2slab[:, dc * 128:(dc + 1) * 128], pt[:, 0:128])
                    E.drain().then_inc(sS.obj, 1)
                sS.val += 1
                ev_done2.append(sS.val)

            wait("scalar", sS, sS.val)
            @on("scalar")
            def _(E, tb=tb):
                E.dma_start(
                    out=lnt_i[tb // 2].rearrange("(m p) c -> p m c", p=128)[:, :, (tb % 2) * 128:(tb % 2) * 128 + 128],
                    in_=z2slab[:].rearrange("p (m c) -> p m c", m=NDC),
                ).then_inc(sLNT.obj, 16)
            sLNT.val += 16
            if tb % 2 == 1:
                k = tb // 2
                wait("gpsimd", sLNT, sLNT.val)
                @on("gpsimd")
                def _(E, k=k):
                    E.collective_compute(
                        "AllGather", ALU.bypass,
                        replica_groups=[[0, 1], [2, 3], [4, 5], [6, 7]],
                        ins=[lnt_i[k].opt()], outs=[lnt_o[k].opt()],
                    ).then_inc(sCC.obj, 1)
                sCC.val += 1
                agA.append(sCC.val)
        AG_A = sCC.val

        pw_loaded = []
        rhs_used = []
        pj_ev = [(0, 0), (0, 0)]
        for wi, (scr, dst_s, bias_t) in enumerate(
            ((pdt_s, dt_s, bdt_t), (pb_s, b_s, bb_t), (pc_s, c_s, bc_t))
        ):
            if wi == 0:
                wait("sync", sPCAST, PROJ_CAST)
            if wi >= 2:
                wait("sync", sT, pw_loaded[wi - 2])
            psem = sPW[wi % 2]
            @on("sync")
            def _(E, scr=scr, psem=psem, wi=wi):
                for dc in range(NDC):
                    E.dma_start(out=pwfull[wi % 2][:, dc * CH:(dc + 1) * CH],
                                in_=scr[dc * 128:(dc + 1) * 128, :]).then_inc(psem.obj, 16)
            psem.val += 16 * NDC
            PW_OK = psem.val

            for ttile in range(4):
                k = wi * 4 + ttile
                slot, toff = ttile // 2, (ttile % 2) * 512
                if k >= 2:
                    wait("gpsimd", sT, rhs_used[k - 2])
                rsem = sRHSp[ttile % 2]
                ka = 2 * (ttile % 2)
                wait("gpsimd", sCC, agA[ka + 1])
                @on("gpsimd")
                def _(E, slot=slot, ka=ka, ttile=ttile, rsem=rsem):
                    for dc in range(NDC):
                        E.dma_start(
                            out=rhsT2[ttile % 2][:, dc * 512:(dc + 1) * 512].rearrange("p (k c) -> p k c", k=2),
                            in_=lnt_o[ka:ka + 2, slot, dc * 128:(dc + 1) * 128, :].rearrange("k p c -> p k c"),
                        ).then_inc(rsem.obj, 16)
                rsem.val += 16 * NDC

                wait("tensor", rsem, rsem.val)
                wait("tensor", psem, PW_OK)
                MM_OKS = []
                for ehalf in range(2):
                    wait("tensor", sV, pj_ev[ehalf][0])
                    wait("tensor", sS, pj_ev[ehalf][1])
                    @on("tensor")
                    def _(E, wi=wi, ttile=ttile, ehalf=ehalf):
                        mm = None
                        for dc in range(NDC):
                            for eb in range(ehalf * 4, ehalf * 4 + 4):
                                pd = (ps_d + ps_g)[eb]
                                mm = E.matmul(pd[:, 0:512],
                                              pwfull[wi % 2][:, dc * CH + eb * 128: dc * CH + (eb + 1) * 128],
                                              rhsT2[ttile % 2][:, dc * 512:(dc + 1) * 512],
                                              start=(dc == 0), stop=(dc == NDC - 1))
                        mm.then_inc(sT.obj, 1)
                    sT.val += 1
                    MM_OKS.append(sT.val)
                MM_OK = sT.val
                rhs_used.append(sT.val)
                if ttile == 3:
                    pw_loaded.append(sT.val)

                stt_done = []
                for eb in range(NEB):
                    pd = (ps_d + ps_g)[eb]
                    stg = pstage[eb % 2]
                    if wi == 0:
                        wait("scalar", sT, MM_OKS[eb // 4])
                        if eb >= 1:
                            wait("scalar", sV, stt_done[eb - 1])
                        @on("scalar")
                        def _(E, eb=eb, pd=pd, bias_t=bias_t):
                            E.activation(t3_t[:], pd[:, 0:512], AF.Abs,
                                         bias=bias_t[:, eb:eb + 1], scale=1.0)
                            E.drain()
                            E.activation(t3_t[:], t3_t[:], AF.Exp, bias=0.0, scale=-1.0)
                            E.drain()
                            E.activation(t3_t[:], t3_t[:], AF.Ln,
                                         bias=consts[:, CONES:CONES + 1], scale=1.0)
                            E.drain()
                            E.activation(t4_t[:], pd[:, 0:512], AF.Relu,
                                         bias=bias_t[:, eb:eb + 1], scale=1.0)
                            E.drain().then_inc(sS.obj, 1)
                        sS.val += 1

                        wait("vector", sS, sS.val)
                        wait("vector", sPJS[eb % 2], sPJS[eb % 2].val)
                        @on("vector")
                        def _(E, stg=stg):
                            E.scalar_tensor_tensor(stg[:], t3_t[:], DT_MIN,
                                                   t4_t[:], ALU.add, ALU.add)
                            E.drain().then_inc(sV.obj, 1)
                        sV.val += 1
                        stt_done.append(sV.val)
                        wait("scalar", sV, sV.val)
                    else:
                        wait("vector", sT, MM_OKS[eb // 4])
                        wait("vector", sPJS[eb % 2], sPJS[eb % 2].val)
                        @on("vector")
                        def _(E, eb=eb, pd=pd, stg=stg, bias_t=bias_t):
                            E.tensor_scalar(stg[:], pd[:, 0:512],
                                            bias_t[:, eb:eb + 1], None, ALU.add)
                            E.drain().then_inc(sV.obj, 1)
                        sV.val += 1
                        wait("scalar", sV, sV.val)
                    @on("scalar")
                    def _(E, eb=eb, stg=stg, dst_s=dst_s, ttile=ttile):
                        E.dma_start(out=dst_s[eb * 128:(eb + 1) * 128,
                                              ttile * 512:(ttile + 1) * 512],
                                    in_=stg[:]).then_inc(sPJS[eb % 2].obj, 16)
                    sPJS[eb % 2].val += 16
                    if eb == 3 or eb == 7:
                        pj_ev[eb // 4] = (sV.val, sS.val)

        full_barrier()
        mp_ctx.close()

        u_c = sb("u_c", [128, S], F32, m_ctx)
        dtb = [sb(f"dtb{i}", [128, S], F32, m_ctx) for i in range(2)]
        bb2 = sb("bb2", [128, S], F32, m_ctx)
        cb2 = sb("cb2", [128, S], F32, m_ctx)
        a_sl = [sb(f"a_sl{i}", [128, 2 * S], F32, m_ctx) for i in range(2)]
        xs_sl = sb("xs_sl", [128, 2 * S], F32, m_ctx)
        acc2 = [sb(f"acc2{i}", [128, S], F32, m_ctx) for i in range(2)]
        y_eb = sb("y_eb", [128, S], F32, m_ctx)
        g_t = sb("g_t", [128, S], F32, m_ctx)

        yld = sb("yld", [128, TOK], F32, m_ctx)
        ysel = sb("ysel", [128, TOK], F32, m_ctx)
        agB = []
        yr_state = {"sel": 0, "tr": 0}

        def emit_yresid(dc):
            slot, row = dc // NEB, (dc % NEB) * 128
            wait("sync", sCC, agB[dc % NEB])
            if yr_state["sel"]:
                wait("sync", sV, yr_state["sel"])
            @on("sync")
            def _(E, slot=slot, row=row):
                E.dma_start(out=yld[:],
                            in_=y_o[row // 128, slot, :, 0:TOK]).then_inc(sYLD.obj, 16)
            sYLD.val += 16

            wait("vector", sYLD, sYLD.val)
            if yr_state["tr"]:
                wait("vector", sT, yr_state["tr"])
            @on("vector")
            def _(E):
                E.tensor_scalar(ysel[:], yld[:], hmask_t[:, 0:1], None, ALU.mult)
                E.drain().then_inc(sV.obj, 1)
            sV.val += 1

            wait("sync", sV, sV.val)
            @on("sync")
            def _(E, slot=slot, row=row):
                E.dma_start(out=yld[:],
                            in_=y_o[row // 128, slot, :, TOK:2 * TOK]).then_inc(sYLD.obj, 16)
            sYLD.val += 16

            wait("vector", sYLD, sYLD.val)
            @on("vector")
            def _(E):
                E.scalar_tensor_tensor(ysel[:], yld[:], hmask_t[:, 1:2],
                                       ysel[:], ALU.mult, ALU.add)
                E.drain().then_inc(sV.obj, 1)
            sV.val += 1
            yr_state["sel"] = sV.val
            YSEL = sV.val

            ytr_ev = []
            for tb in range(NTB):
                pt = ps_g[tb % 2]
                wait("tensor", sV, YSEL)
                if tb >= 2:
                    wait("tensor", sV, ytr_ev[tb - 2])
                @on("tensor")
                def _(E, tb=tb, pt=pt):
                    E.transpose(pt[:, 0:128], ysel[:, tb * 128:(tb + 1) * 128],
                                id32_t[:]).then_inc(sT.obj, 1)
                sT.val += 1

                wait("vector", sT, sT.val)
                @on("vector")
                def _(E, tb=tb, dc=dc, pt=pt):
                    E.scalar_tensor_tensor(
                        x_tm[:, tb * D + dc * 128: tb * D + (dc + 1) * 128],
                        pt[:, 0:128], 1.0,
                        x_tm[:, tb * D + dc * 128: tb * D + (dc + 1) * 128],
                        ALU.bypass, ALU.add)
                    E.drain().then_inc(sV.obj, 1)
                sV.val += 1
                ytr_ev.append(sV.val)
            yr_state["tr"] = sT.val

        NP = N // 2
        scan_np = [0, 0]
        gp_np = [0, 0]
        dt_sca = [0, 0]
        dt_vec = [0, 0]
        for eb in range(NEB):
            par = eb % 2
            if eb >= 2:
                wait("sync", sS, dt_sca[par])
                wait("sync", sV, dt_vec[par])
            if eb >= 1:
                wait("sync", sGPC, sGPC.val)
            csem = sDT[par]
            @on("sync")
            def _(E, eb=eb, par=par, csem=csem):
                E.dma_start(out=dtb[par][:], in_=dt_s[eb * 128:(eb + 1) * 128, :]).then_inc(csem.obj, 16)
                E.dma_start(out=bb2[:], in_=b_s[eb * 128:(eb + 1) * 128, :]).then_inc(csem.obj, 16)
                E.dma_start(out=cb2[:], in_=c_s[eb * 128:(eb + 1) * 128, :]).then_inc(csem.obj, 16)
            csem.val += 48
            CH_OK = csem.val

            wait("sync", sCC, AG_A)
            if eb >= 2:
                wait("sync", sV, scan_np[par])
            @on("sync")
            def _(E, eb=eb, par=par):
                for sl2 in range(2):
                    E.dma_start(
                        out=a_sl[par][:, sl2 * TOK: (sl2 + 1) * TOK].rearrange("p (k t) -> p k t", t=256),
                        in_=lnt_o[:, sl2, eb * 128:(eb + 1) * 128, :].rearrange("k p t -> p k t"),
                    ).then_inc(sUL.obj, 16)
                    E.dma_start(
                        out=a_sl[par][:, S + sl2 * TOK: S + (sl2 + 1) * TOK].rearrange("p (k t) -> p k t", t=256),
                        in_=lnt_o[:, sl2, CH + eb * 128: CH + (eb + 1) * 128, :].rearrange("k p t -> p k t"),
                    ).then_inc(sUL.obj, 16)
            sUL.val += 64
            U_OK = sUL.val

            wait("vector", sUL, U_OK)
            wait("vector", csem, CH_OK)
            if eb >= 1:
                wait("vector", sGPC, sGPC.val)
                wait("vector", sYST, sYST.val)
            @on("vector")
            def _(E, par=par, eb=eb):
                E.tensor_scalar(u_c[:], a_sl[par][:, S:2 * S], hmask_t[:, 1:2], None, ALU.mult)
                E.drain()
                E.scalar_tensor_tensor(u_c[:], a_sl[par][:, 0:S], hmask_t[:, 0:1],
                                       u_c[:], ALU.mult, ALU.add)
                E.drain()
                E.tensor_tensor(g_t[:], dtb[par][:], bb2[:], ALU.mult)
                E.drain()
                E.tensor_tensor(g_t[:], g_t[:], u_c[:], ALU.mult)
                E.drain()
                E.tensor_scalar(u_c[:], u_c[:], dv_t[:, eb:eb + 1], None, ALU.mult)
                E.drain().then_inc(sV.obj, 1)
            sV.val += 1
            MASK_OK = sV.val
            dt_vec[par] = MASK_OK

            scan_n = [0, 0]
            for n in range(N):
                apar = n % 2
                ppar = (n // 2) % 2
                wait("scalar", sV, MASK_OK)
                if n >= 2:
                    wait("scalar", sV, scan_n[apar])
                @on("scalar")
                def _(E, eb=eb, par=par, n=n, apar=apar):
                    E.activation(a_sl[apar][:, 0:S], dtb[par][:],
                                 AF.Exp, bias=0.0,
                                 scale=a_t[:, eb * N + n: eb * N + n + 1])
                    E.drain()
                    E.activation(a_sl[apar][:, S:2 * S], g_t[:], AF.Copy,
                                 bias=0.0,
                                 scale=btl_t[:, eb * N + n: eb * N + n + 1])
                    E.drain().then_inc(sS.obj, 1)
                sS.val += 1
                A_OK = sS.val
                if n == N - 1:
                    dt_sca[par] = sS.val

                @on("vector")
                def _(E, v=A_OK):
                    E.wait_ge(sS.obj, v)
                if n % 2 == 1 and n >= 3:
                    wait("vector", sGPC, gp_np[ppar])
                if n % 2 == 0 and n >= 2:
                    wait("vector", sGPC, gp_np[0])
                @on("vector")
                def _(E, n=n, apar=apar):
                    E.tensor_tensor_scan(
                        xs_sl[:, apar * S:(apar + 1) * S],
                        a_sl[apar][:, 0:S],
                        a_sl[apar][:, S:2 * S],
                        0.0, ALU.mult, ALU.add)
                    E.drain().then_inc(sV.obj, 1)
                sV.val += 1
                scan_n[apar] = sV.val

                if n % 2 == 1:
                    wait("gpsimd", sV, scan_n[0])
                    wait("gpsimd", sV, scan_n[1])
                    @on("gpsimd")
                    def _(E, n=n):
                        if n == 1:
                            E.tensor_tensor(y_eb[:], xs_sl[:, 0:S],
                                            xs_sl[:, S:2 * S], ALU.add)
                        else:
                            E.tensor_tensor(y_eb[:], y_eb[:], xs_sl[:, 0:S], ALU.add)
                            E.drain()
                            E.tensor_tensor(y_eb[:], y_eb[:], xs_sl[:, S:2 * S], ALU.add)
                        E.drain().then_inc(sGPC.obj, 1)
                    sGPC.val += 1
                    gp_np[0] = sGPC.val

            @on("gpsimd")
            def _(E):
                E.tensor_tensor(y_eb[:], y_eb[:], cb2[:], ALU.mult)
                E.drain()
                E.tensor_tensor(y_eb[:], y_eb[:], u_c[:], ALU.add)
                E.drain().then_inc(sGPC.obj, 1)
            sGPC.val += 1

            wait("scalar", sGPC, sGPC.val)
            @on("scalar")
            def _(E, eb=eb):
                E.dma_start(out=y_i[eb], in_=y_eb[:]).then_inc(sYST.obj, 16)
            sYST.val += 16

            wait("gpsimd", sYST, sYST.val)
            @on("gpsimd")
            def _(E, eb=eb):
                E.collective_compute(
                    "AllGather", ALU.bypass,
                    replica_groups=[[0, 1], [2, 3], [4, 5], [6, 7]],
                    ins=[y_i[eb].opt()], outs=[y_o[eb].opt()],
                ).then_inc(sCC.obj, 1)
            sCC.val += 1
            agB.append(sCC.val)
            if eb >= 1:
                emit_yresid(eb - 1)
                emit_yresid(eb - 1 + NEB)
        emit_yresid(NEB - 1)
        emit_yresid(2 * NEB - 1)
        AG_B = sCC.val

        pass

    full_barrier()
    m_ctx.close()
    if stage >= 3:
        f2_ctx = ExitStack()
        f2_bufs = FfnBufs("f2", f2_ctx)
        emit_ffn(1, f2_bufs, make_quant(f2_bufs))
        f2_ctx.close()

    wait("sync", sV, sV.val)
    @on("sync")
    def _(E):
        for tb in range(NTB):
            E.dma_start(out=out_d[tb * 128:(tb + 1) * 128, :],
                        in_=x_tm[:, tb * D:(tb + 1) * D]).then_inc(sOUT.obj, 16)
    sOUT.val += 16 * NTB

    wait("gpsimd", sOUT, sOUT.val)

    with ctx:
        with ExitStack() as sctx:
            for s in all_sems:
                s.obj = sctx.enter_context(nc.semaphore(s.name))
            with nc.Block() as block:
                @block.sync
                def _(E):
                    for fn in steps["sync"]:
                        fn(E)

                @block.scalar
                def _(E):
                    for fn in steps["scalar"]:
                        fn(E)

                @block.vector
                def _(E):
                    for fn in steps["vector"]:
                        fn(E)

                @block.tensor
                def _(E):
                    for fn in steps["tensor"]:
                        fn(E)

                @block.gpsimd
                def _(E):
                    for fn in steps["gpsimd"]:
                        fn(E)
    return nc


_NC_CACHE = None


def kernel(**inputs):
    global _NC_CACHE
    import ml_dtypes

    x = np.asarray(inputs["L0_attn_u"], dtype=np.float32)
    wg = np.asarray(inputs["L0_ffn_W_g"], dtype=np.float32)
    wu = np.asarray(inputs["L0_ffn_W_u"], dtype=np.float32)
    wd = np.asarray(inputs["L0_ffn_W_d"], dtype=np.float32)
    wdt = np.asarray(inputs["L0_attn_Wdt"], dtype=np.float32)
    wb = np.asarray(inputs["L0_attn_Wb"], dtype=np.float32)
    wc = np.asarray(inputs["L0_attn_Wc"], dtype=np.float32)
    bdt = np.asarray(inputs["L0_attn_bdt"], dtype=np.float32)
    bb = np.asarray(inputs["L0_attn_bb"], dtype=np.float32)
    bc = np.asarray(inputs["L0_attn_bc"], dtype=np.float32)
    araw = np.asarray(inputs["L0_attn_a_raw"], dtype=np.float32)
    bbase = np.asarray(inputs["L0_attn_B_base"], dtype=np.float32)
    cbase = np.asarray(inputs["L0_attn_C_base"], dtype=np.float32)
    dvec = np.asarray(inputs["L0_attn_D"], dtype=np.float32)

    id32 = np.eye(128, dtype=np.float32)
    id16 = np.eye(128, dtype=ml_dtypes.bfloat16)

    if _NC_CACHE is None:
        _NC_CACHE = build_nc()
    nc = _NC_CACHE

    in_maps = []
    for c in range(8):
        b, half = c // 2, c % 2
        lo, hi = half * CH, (half + 1) * CH
        hm = np.zeros((128, 2), np.float32)
        hm[:, 0] = 1.0 if half == 0 else 0.0
        hm[:, 1] = 1.0 - hm[:, 0]
        in_maps.append({
            "x": np.ascontiguousarray(x[b, half * TOK:(half + 1) * TOK, :]),
            "wg": wg, "wu": wu, "wd": wd,
            "wdt": np.ascontiguousarray(wdt[:, lo:hi]),
            "wb": np.ascontiguousarray(wb[:, lo:hi]),
            "wc": np.ascontiguousarray(wc[:, lo:hi]),
            "bdt": np.ascontiguousarray(bdt[lo:hi]),
            "bb": np.ascontiguousarray(bb[lo:hi]),
            "bc": np.ascontiguousarray(bc[lo:hi]),
            "araw": np.ascontiguousarray(araw[lo:hi]),
            "bbase": np.ascontiguousarray(bbase[lo:hi]),
            "cbase": np.ascontiguousarray(cbase[lo:hi]),
            "dvec": np.ascontiguousarray(dvec[lo:hi]),
            "id16": id16, "id32": id32, "hmask": hm,
        })

    res = run_bass_kernel_spmd(nc, in_maps, core_ids=list(range(8)))
    out = np.empty((B, S, D), np.float32)
    for c in range(8):
        b, half = c // 2, c % 2
        out[b, half * TOK:(half + 1) * TOK, :] = res.results[c]["out"]
    return out
